# version 01
# baseline (speedup 1.0000x reference)
import numpy as np
import ml_dtypes
from contextlib import ExitStack
import concourse.bass as bass
import concourse.mybir as mybir
from concourse.bass_utils import run_bass_kernel_spmd

bfnp = ml_dtypes.bfloat16
F32 = mybir.dt.float32
BF16 = mybir.dt.bfloat16
AF = mybir.ActivationFunctionType
ALU = mybir.AluOpType
AX = mybir.AxisListType

D = 2048
T = 2048
HF = 8192
H2 = 2 * HF
NH, DK, BS, NB = 32, 64, 128, 16
NDC = D // 128
NHC = HF // 128
ENGS = ["sync", "gpsimd", "tensor", "vector", "scalar"]
MARKS = {"tensor": "mT", "vector": "mV", "scalar": "mS", "sync": "mY",
         "gpsimd": "mG"}


class Env:
    def __init__(self, nc, block, ctx):
        self.nc = nc
        self.block = block
        self.ctx = ctx
        self.n = {}
        self.sem = {}
        for name in ["dS", "dG", "dC", "tE", "vE", "sE", "gE",
                     "mT", "mV", "mS", "mY", "mG"]:
            self.sem[name] = ctx.enter_context(nc.semaphore(name))
            self.n[name] = 0
        self.phase = 0
        self.dbg = {}

    def inc(self, name, inst, k=None):
        if k is None:
            k = 16 if name.startswith("d") else 1
        inst.then_inc(self.sem[name], k)
        self.n[name] += k
        return self.n[name]

    def w(self, eng, name, val):
        eng.wait_ge(self.sem[name], val)

    def run_phase(self, fns):
        for kind in ENGS:
            fn = fns.get(kind)
            dec = getattr(self.block, kind)

            def body(eng, fn=fn, kind=kind):
                for k2, m in MARKS.items():
                    if k2 != kind:
                        self.w(eng, m, self.phase)
                if fn is not None:
                    fn(eng)
                if kind in ("vector", "scalar", "gpsimd"):
                    eng.drain()
                self.inc(MARKS[kind], eng.nop(), 1)

            dec(body)
        self.phase += 1
        for m in MARKS.values():
            self.n[m] = self.phase


def emit_ln(E, C, x_src, xn):
    nc = E.nc
    with ExitStack() as ph:
        x2 = [ph.enter_context(nc.sbuf_tensor(f"x2_{E.phase}_{i}", [128, T], BF16))
              for i in range(2)]
        st = [ph.enter_context(nc.sbuf_tensor(f"lnst{E.phase}_{i}", [1, T], F32))
              for i in range(4)]
        ps_scope = ExitStack()
        ps_sum = [ps_scope.enter_context(
            nc.psum_tensor(f"psst{E.phase}_{i}", [1, 512], F32))
            for i in range(8)]
        b_dG, b_vE, b_sE, b_tE = E.n["dG"], E.n["vE"], E.n["sE"], E.n["tE"]

        def f_gpsimd(g):
            for dc in range(NDC):
                E.inc("dG", g.dma_start(out=xn[:, dc * T:(dc + 1) * T],
                                        in_=x_src[dc * 128:(dc + 1) * 128, :]))

        def f_scalar(s):
            for dc in range(NDC):
                E.w(s, "dG", b_dG + 16 * (dc + 1))
                if dc >= 2:
                    E.w(s, "tE", b_tE + 2 * (dc - 1))
                E.inc("sE", s.activation(out=x2[dc % 2][:], func=AF.Square,
                                         in_=xn[:, dc * T:(dc + 1) * T]))
            E.w(s, "vE", b_vE + 12)
            E.inc("sE", s.activation(out=st[2][:], in_=st[1][:], func=AF.Sqrt))

        def f_tensor(t):
            for dc in range(NDC):
                E.w(t, "dG", b_dG + 16 * (dc + 1))
                E.w(t, "sE", b_sE + dc + 1)
                for nt in range(4):
                    t.matmul(out=ps_sum[nt][0:1, :], lhsT=C["ones_col_bf"][:],
                             rhs=xn[:, dc * T + nt * 512: dc * T + (nt + 1) * 512],
                             start=(dc == 0), stop=(dc == NDC - 1))
                for nt in range(4):
                    mm = t.matmul(out=ps_sum[4 + nt][0:1, :],
                                  lhsT=C["ones_col_bf"][:],
                                  rhs=x2[dc % 2][:, nt * 512:(nt + 1) * 512],
                                  start=(dc == 0), stop=(dc == NDC - 1))
                E.inc("tE", mm, 2)

        def f_vector(v):
            E.w(v, "tE", b_tE + 2 * NDC)
            for nt in range(4):
                v.tensor_copy(out=st[0][0:1, nt * 512:(nt + 1) * 512],
                              in_=ps_sum[nt][0:1, :])
            for nt in range(4):
                v.tensor_copy(out=st[1][0:1, nt * 512:(nt + 1) * 512],
                              in_=ps_sum[4 + nt][0:1, :])
            v.drain()
            E.inc("vE", v.engine_nop(), 8)
            v.tensor_scalar_mul(out=st[0][:], in0=st[0][:], scalar1=float(1.0 / D))
            v.drain()
            v.tensor_scalar_mul(out=st[1][:], in0=st[1][:], scalar1=float(1.0 / D))
            v.drain()
            v.tensor_tensor(out=st[2][:], in0=st[0][:], in1=st[0][:], op=ALU.mult)
            v.drain()
            v.tensor_tensor(out=st[1][:], in0=st[1][:], in1=st[2][:],
                            op=ALU.subtract)
            v.drain()
            v.tensor_scalar_add(out=st[1][:], in0=st[1][:], scalar1=1e-5)
            v.drain()
            E.inc("vE", v.engine_nop(), 4)
            E.w(v, "sE", b_sE + NDC + 1)
            v.reciprocal(out=st[2][:], in_=st[2][:])
            v.drain()
            v.tensor_tensor(out=st[3][:], in0=st[0][:], in1=st[2][:], op=ALU.mult)
            v.drain()
            v.tensor_scalar_mul(out=st[3][:], in0=st[3][:], scalar1=-1.0)
            v.drain()
            E.inc("vE", v.engine_nop(), 1)

        E.run_phase({"gpsimd": f_gpsimd, "scalar": f_scalar,
                     "tensor": f_tensor, "vector": f_vector})
        ps_scope.close()

        with ExitStack() as ph2:
            ps_bc = [ph2.enter_context(
                nc.psum_tensor(f"psbc{E.phase}_{i}", [128, 512], F32))
                for i in range(2)]
            A = ph2.enter_context(nc.sbuf_tensor(f"Abc{E.phase}", [128, T], F32))
            Bt = ph2.enter_context(nc.sbuf_tensor(f"Bbc{E.phase}", [128, T], F32))
            tmp = [ph2.enter_context(
                nc.sbuf_tensor(f"lntmp{E.phase}_{i}", [128, T], F32))
                for i in range(2)]
            b_vE2, b_tE2 = E.n["vE"], E.n["tE"]

            def f2_tensor(t):
                for nt in range(4):
                    if nt >= 1:
                        E.w(t, "vE", b_vE2 + 2 * nt)
                    t.matmul(out=ps_bc[0][:], lhsT=C["ones_row_f32"][:],
                             rhs=st[2][0:1, nt * 512:(nt + 1) * 512],
                             start=True, stop=True)
                    mm = t.matmul(out=ps_bc[1][:], lhsT=C["ones_row_f32"][:],
                                  rhs=st[3][0:1, nt * 512:(nt + 1) * 512],
                                  start=True, stop=True)
                    E.inc("tE", mm, 1)

            def f2_vector(v):
                for nt in range(4):
                    E.w(v, "tE", b_tE2 + nt + 1)
                    v.tensor_copy(out=A[:, nt * 512:(nt + 1) * 512], in_=ps_bc[0][:])
                    inst = v.tensor_copy(out=Bt[:, nt * 512:(nt + 1) * 512],
                                         in_=ps_bc[1][:])
                    v.drain()
                    E.inc("vE", v.engine_nop(), 2)
                for dc in range(NDC):
                    v.tensor_tensor(out=tmp[dc % 2][:],
                                    in0=xn[:, dc * T:(dc + 1) * T], in1=A[:],
                                    op=ALU.mult)
                    v.drain()
                    v.tensor_tensor(out=xn[:, dc * T:(dc + 1) * T],
                                    in0=tmp[dc % 2][:], in1=Bt[:], op=ALU.add)
                    v.drain()
                    E.inc("vE", v.engine_nop(), 1)

            E.run_phase({"tensor": f2_tensor, "vector": f2_vector})


def emit_quant(E, C, blocks, dst_info):
    nc = E.nc
    NBLK = len(blocks)
    with ExitStack() as ph:
        wf = [ph.enter_context(nc.sbuf_tensor(f"qwf{E.phase}_{i}", [128, 2048], F32))
              for i in range(2)]
        wb = [ph.enter_context(nc.sbuf_tensor(f"qwb{E.phase}_{i}", [128, 2048], BF16))
              for i in range(2)]
        aw = [ph.enter_context(nc.sbuf_tensor(f"qaw{E.phase}_{i}", [128, 2048], BF16))
              for i in range(2)]
        sg = [ph.enter_context(nc.sbuf_tensor(f"qsg{E.phase}_{i}", [128, 2048], BF16))
              for i in range(2)]
        vt = [ph.enter_context(nc.sbuf_tensor(f"qv{E.phase}_{i}", [128, 2048], BF16))
              for i in range(2)]
        seg = ph.enter_context(nc.sbuf_tensor(f"qseg{E.phase}", [128, 32], F32))
        ssc = ph.enter_context(nc.sbuf_tensor(f"qs{E.phase}", [128, 32], BF16))
        thr = ph.enter_context(nc.sbuf_tensor(f"qth{E.phase}", [128, 32], F32))
        b_dS, b_sE, b_vE, b_dC = E.n["dS"], E.n["sE"], E.n["vE"], E.n["dC"]

        def f_sync(y):
            for i, (src, _) in enumerate(blocks):
                if i >= 2:
                    E.w(y, "sE", b_sE + 2 * (i - 1))
                E.inc("dS", y.dma_start(out=wf[i % 2][:], in_=src))

        def f_scalar(s):
            for i in range(NBLK):
                E.w(s, "dS", b_dS + 16 * (i + 1))
                if i >= 2:
                    E.w(s, "vE", b_vE + 5 * (i - 1))
                s.copy(out=wb[i % 2][:], in_=wf[i % 2][:])
                s.drain()
                s.activation(out=aw[i % 2][:], in_=wb[i % 2][:], func=AF.Abs)
                E.inc("sE", s.activation(out=sg[i % 2][:], in_=wb[i % 2][:],
                                         func=AF.Sign), 2)

        def f_vector(v):
            for i in range(NBLK):
                E.w(v, "sE", b_sE + 2 * (i + 1))
                if i >= 2:
                    E.w(v, "dC", b_dC + 16 * (i - 1))
                p = i % 2
                wb3 = wb[p].ap().rearrange("p (g e) -> p g e", e=64)
                aw3 = aw[p].ap().rearrange("p (g e) -> p g e", e=64)
                v.tensor_reduce(out=seg[:], in_=wb3, axis=AX.X, op=ALU.add,
                                apply_absolute_value=True)
                v.drain()
                v.tensor_scalar(out=ssc[:], in0=seg[:], scalar1=float(1.0 / 64),
                                scalar2=1e-8, op0=ALU.mult, op1=ALU.max)
                v.drain()
                v.tensor_scalar_mul(out=thr[:], in0=ssc[:], scalar1=0.501953125)
                v.drain()
                v.tensor_tensor(out=aw3, in0=aw3,
                                in1=thr.ap()[:, :, None].broadcast_to([128, 32, 64]),
                                op=ALU.is_gt)
                v.drain()
                v.tensor_tensor(out=aw[p][:], in0=aw[p][:], in1=sg[p][:],
                                op=ALU.mult)
                v.drain()
                v.tensor_tensor(out=aw3, in0=aw3,
                                in1=ssc.ap()[:, :, None].broadcast_to([128, 32, 64]),
                                op=ALU.mult)
                v.drain()
                v.tensor_tensor(out=aw[p][:], in0=aw[p][:], in1=wb[p][:],
                                op=ALU.subtract)
                v.drain()
                inst = v.tensor_tensor(out=vt[p][:], in0=wb[p][:], in1=aw[p][:],
                                       op=ALU.add)
                E.inc("vE", inst, 5)

        def f_gpsimd(g):
            for i, (_, dst) in enumerate(blocks):
                E.w(g, "vE", b_vE + 5 * (i + 1))
                E.inc("dC", g.dma_start(out=dst, in_=vt[i % 2][:]))

        E.run_phase({"sync": f_sync, "scalar": f_scalar, "vector": f_vector,
                     "gpsimd": f_gpsimd})


def emit_xbar(E, C, jobs, sb_strips=None):
    nc = E.nc
    NJ = len(jobs)
    with ExitStack() as ph:
        own = sb_strips is None
        if own:
            R = jobs[0][2]
            sb_strips = [ph.enter_context(
                nc.sbuf_tensor(f"xb{E.phase}_{i}", [128, R], BF16,
                               is_transpose_dest=True)) for i in range(2)]
        b_dS, b_dC = E.n["dS"], E.n["dC"]

        def f_sync(y):
            for i, (src, dst, R) in enumerate(jobs):
                buf = sb_strips[i % 2] if own else sb_strips[i]
                if own and i >= 2:
                    E.w(y, "dC", b_dC + 16 * (i - 1))
                E.inc("dS", y.dma_start(out=buf[:, 0:R], in_=src, transpose=True))

        def f_scalar(s):
            for i, (src, dst, R) in enumerate(jobs):
                if dst is None:
                    continue
                buf = sb_strips[i % 2] if own else sb_strips[i]
                E.w(s, "dS", b_dS + 16 * (i + 1))
                E.inc("dC", s.dma_start(out=dst, in_=buf[:, 0:R]))

        E.run_phase({"sync": f_sync, "scalar": f_scalar})


def emit_gu(E, C, xn, wguT_io, h_io, ps_ssq):
    nc = E.nc
    with ExitStack() as ph:
        gpk = [ph.enter_context(nc.sbuf_tensor(f"gpk{E.phase}_{i}", [128, 2048], BF16))
               for i in range(2)]
        upk = [ph.enter_context(nc.sbuf_tensor(f"upk{E.phase}_{i}", [128, 2048], BF16))
               for i in range(2)]
        hj = [ph.enter_context(nc.sbuf_tensor(f"hj{E.phase}_{i}", [128, 2048], BF16))
              for i in range(2)]
        h2t = [ph.enter_context(nc.sbuf_tensor(f"h2t{E.phase}_{i}", [128, 2048], BF16))
               for i in range(2)]
        stmp = [ph.enter_context(nc.sbuf_tensor(f"stmp{E.phase}_{i}", [128, 512], F32))
                for i in range(2)]
        pg = [ph.enter_context(nc.psum_tensor(f"pg{E.phase}_{i}", [128, 512], F32))
              for i in range(2)]
        pu = [ph.enter_context(nc.psum_tensor(f"pu{E.phase}_{i}", [128, 512], F32))
              for i in range(2)]
        b_dS, b_dG, b_dC = E.n["dS"], E.n["dG"], E.n["dC"]
        b_tE, b_vE, b_sE = E.n["tE"], E.n["vE"], E.n["sE"]
        NJ = NHC

        def t_gu(j, nt):
            return b_tE + (nt + 1 if j == 0 else 8 * j - 4 + nt + 1)

        def t_gu_end(j):
            return b_tE + (4 if j == 0 else 8 * j)

        def f_sync(y):
            for j in range(NJ):
                if j >= 2:
                    E.w(y, "tE", t_gu_end(j - 2))
                E.inc("dS", y.dma_start(
                    out=gpk[j % 2].ap().rearrange("p (dc c) -> p dc c", c=128),
                    in_=wguT_io[:, :, j * 128:(j + 1) * 128].rearrange(
                        "dc p c -> p dc c")))

        def f_gpsimd(g):
            for j in range(NJ):
                if j >= 2:
                    E.w(g, "tE", t_gu_end(j - 2))
                E.inc("dG", g.dma_start(
                    out=upk[j % 2].ap().rearrange("p (dc c) -> p dc c", c=128),
                    in_=wguT_io[:, :, HF + j * 128: HF + (j + 1) * 128].rearrange(
                        "dc p c -> p dc c")))

        def f_tensor(t):
            for j in range(NJ):
                E.w(t, "dS", b_dS + 16 * (j + 1))
                E.w(t, "dG", b_dG + 16 * (j + 1))
                for nt in range(4):
                    i = 4 * j + nt
                    if i >= 2:
                        E.w(t, "sE", b_sE + i - 1)
                        E.w(t, "vE", b_vE + i - 1)
                    for dc in range(NDC):
                        t.matmul(out=pg[i % 2][:],
                                 lhsT=gpk[j % 2][:, dc * 128:(dc + 1) * 128],
                                 rhs=xn[:, dc * T + nt * 512: dc * T + (nt + 1) * 512],
                                 start=(dc == 0), stop=(dc == NDC - 1))
                    for dc in range(NDC):
                        mm = t.matmul(out=pu[i % 2][:],
                                      lhsT=upk[j % 2][:, dc * 128:(dc + 1) * 128],
                                      rhs=xn[:, dc * T + nt * 512: dc * T + (nt + 1) * 512],
                                      start=(dc == 0), stop=(dc == NDC - 1))
                    E.inc("tE", mm, 1)
                if j >= 1:
                    for nt in range(4):
                        E.w(t, "vE", b_vE + 4 * (j - 1) + nt + 1)
                        mm = t.matmul(out=ps_ssq[nt][0:1, :],
                                      lhsT=C["ones_col_bf"][:],
                                      rhs=h2t[(j - 1) % 2][:, nt * 512:(nt + 1) * 512],
                                      start=(j - 1 == 0), stop=False)
                        E.inc("tE", mm, 1)
            for nt in range(4):
                E.w(t, "vE", b_vE + 4 * (NJ - 1) + nt + 1)
                mm = t.matmul(out=ps_ssq[nt][0:1, :], lhsT=C["ones_col_bf"][:],
                              rhs=h2t[(NJ - 1) % 2][:, nt * 512:(nt + 1) * 512],
                              start=False, stop=True)
                E.inc("tE", mm, 1)

        def f_scalar(s):
            for j in range(NJ):
                if j >= 1:
                    E.w(s, "vE", b_vE + 4 * j)
                    E.inc("dC", s.dma_start(
                        out=h_io[(j - 1) * 128: j * 128, :],
                        in_=hj[(j - 1) % 2][:]))
                for nt in range(4):
                    i = 4 * j + nt
                    E.w(s, "tE", t_gu(j, nt))
                    if i >= 2:
                        E.w(s, "vE", b_vE + i - 1)
                    E.inc("sE", s.activation(out=stmp[i % 2][:],
                                             in_=pg[i % 2][:], func=AF.Silu))
            E.w(s, "vE", b_vE + 4 * NJ)
            E.inc("dC", s.dma_start(out=h_io[(NJ - 1) * 128: NJ * 128, :],
                                    in_=hj[(NJ - 1) % 2][:]))

        def f_vector(v):
            for j in range(NJ):
                if j >= 2:
                    E.w(v, "dC", b_dC + 16 * (j - 1))
                for nt in range(4):
                    i = 4 * j + nt
                    E.w(v, "sE", b_sE + i + 1)
                    v.tensor_tensor(out=hj[j % 2][:, nt * 512:(nt + 1) * 512],
                                    in0=stmp[i % 2][:], in1=pu[i % 2][:],
                                    op=ALU.mult)
                    v.drain()
                    v.tensor_tensor(out=h2t[j % 2][:, nt * 512:(nt + 1) * 512],
                                    in0=hj[j % 2][:, nt * 512:(nt + 1) * 512],
                                    in1=hj[j % 2][:, nt * 512:(nt + 1) * 512],
                                    op=ALU.mult)
                    v.drain()
                    E.inc("vE", v.engine_nop(), 1)

        E.run_phase({"sync": f_sync, "gpsimd": f_gpsimd, "tensor": f_tensor,
                     "scalar": f_scalar, "vector": f_vector})


def emit_rms(E, C, ps_ssq, Sb, scale):
    nc = E.nc
    eps = float(np.finfo(np.float32).eps)
    with ExitStack() as ph:
        stv = ph.enter_context(nc.sbuf_tensor(f"rms{E.phase}", [1, T], F32))
        ps_bc = ph.enter_context(nc.psum_tensor(f"psrb{E.phase}", [128, 512], F32))
        b_vE, b_sE, b_tE = E.n["vE"], E.n["sE"], E.n["tE"]

        def f_vector(v):
            for nt in range(4):
                v.tensor_copy(out=stv[0:1, nt * 512:(nt + 1) * 512],
                              in_=ps_ssq[nt][0:1, :])
            v.drain()
            v.tensor_scalar(out=stv[:], in0=stv[:], scalar1=float(1.0 / HF),
                            scalar2=eps, op0=ALU.mult, op1=ALU.add)
            v.drain()
            E.inc("vE", v.engine_nop(), 1)
            E.w(v, "sE", b_sE + 1)
            v.reciprocal(out=stv[:], in_=stv[:])
            v.drain()
            v.tensor_scalar_mul(out=stv[:], in0=stv[:], scalar1=float(scale))
            v.drain()
            E.inc("vE", v.engine_nop(), 1)
            for nt in range(4):
                E.w(v, "tE", b_tE + nt + 1)
                v.tensor_copy(out=Sb[:, nt * 512:(nt + 1) * 512], in_=ps_bc[:])
                v.drain()
                E.inc("vE", v.engine_nop(), 1)

        def f_scalar(s):
            E.w(s, "vE", b_vE + 1)
            E.inc("sE", s.activation(out=stv[:], in_=stv[:], func=AF.Sqrt))

        def f_tensor(t):
            E.w(t, "vE", b_vE + 2)
            for nt in range(4):
                if nt >= 1:
                    E.w(t, "vE", b_vE + 2 + nt)
                mm = t.matmul(out=ps_bc[:], lhsT=C["ones_row_f32"][:],
                              rhs=stv[0:1, nt * 512:(nt + 1) * 512],
                              start=True, stop=True)
                E.inc("tE", mm, 1)

        E.run_phase({"vector": f_vector, "scalar": f_scalar, "tensor": f_tensor})


def emit_proj(E, C, h_io, wpT_io, Sb, x_src, x_dst):
    nc = E.nc
    with ExitStack() as ph:
        hpan = ph.enter_context(nc.sbuf_tensor(f"hpan{E.phase}", [128, 65536], BF16))
        wpk = [ph.enter_context(nc.sbuf_tensor(f"wpk{E.phase}_{i}", [128, 8192], BF16))
               for i in range(2)]
        xch = [ph.enter_context(nc.sbuf_tensor(f"xch{E.phase}_{i}", [128, 1024], F32))
               for i in range(2)]
        xout = [ph.enter_context(nc.sbuf_tensor(f"xo{E.phase}_{i}", [128, 1024], F32))
                for i in range(2)]
        tmp = [ph.enter_context(nc.sbuf_tensor(f"ptmp{E.phase}_{i}", [128, 512], F32))
               for i in range(2)]
        ps_y = [ph.enter_context(nc.psum_tensor(f"psy{E.phase}_{i}", [128, 512], F32))
                for i in range(2)]
        b_dS, b_dG, b_dC = E.n["dS"], E.n["dG"], E.n["dC"]
        b_tE, b_vE = E.n["tE"], E.n["vE"]
        hview = h_io.rearrange("(hc p) t -> hc p t", p=128)

        def f_sync(y):
            for half in range(2):
                if half == 1:
                    E.w(y, "tE", b_tE + 32)
                for q in range(4):
                    E.inc("dS", y.dma_start(
                        out=hpan.ap().rearrange("p (hc c) -> p hc c", c=1024)[
                            :, q * 16:(q + 1) * 16],
                        in_=hview[q * 16:(q + 1) * 16, :,
                                  half * 1024:(half + 1) * 1024].rearrange(
                            "hc p c -> p hc c")))
                for dc in range(16):
                    k = half * 16 + dc
                    if k >= 2:
                        E.w(y, "vE", b_vE + 2 * (k - 1))
                    E.inc("dS", y.dma_start(
                        out=xch[dc % 2][:],
                        in_=x_src[dc * 128:(dc + 1) * 128,
                                  half * 1024:(half + 1) * 1024]))

        def f_gpsimd(g):
            for half in range(2):
                for dc in range(16):
                    k = half * 16 + dc
                    if k >= 2:
                        E.w(g, "tE", b_tE + 2 * (k - 1))
                    E.inc("dG", g.dma_start(
                        out=wpk[k % 2].ap().rearrange("p (hc c) -> p hc c", c=128),
                        in_=wpT_io[:, :, dc * 128:(dc + 1) * 128].rearrange(
                            "hc p c -> p hc c")))

        def f_tensor(t):
            for half in range(2):
                E.w(t, "dS", b_dS + 16 * (4 if half == 0 else 24))
                for dc in range(16):
                    k = half * 16 + dc
                    E.w(t, "dG", b_dG + 16 * (k + 1))
                    for sub in range(2):
                        i = 2 * k + sub
                        if i >= 2:
                            E.w(t, "vE", b_vE + i - 1)
                        for hc in range(NHC):
                            mm = t.matmul(
                                out=ps_y[i % 2][:],
                                lhsT=wpk[k % 2][:, hc * 128:(hc + 1) * 128],
                                rhs=hpan[:, hc * 1024 + sub * 512:
                                         hc * 1024 + (sub + 1) * 512],
                                start=(hc == 0), stop=(hc == NHC - 1))
                        E.inc("tE", mm, 1)

        def f_vector(v):
            for half in range(2):
                for dc in range(16):
                    k = half * 16 + dc
                    E.w(v, "dS", b_dS + 16 * (4 + 4 * half + k + 1))
                    if k >= 2:
                        E.w(v, "dC", b_dC + 16 * (k - 1))
                    for sub in range(2):
                        i = 2 * k + sub
                        E.w(v, "tE", b_tE + i + 1)
                        v.tensor_tensor(
                            out=tmp[i % 2][:], in0=ps_y[i % 2][:],
                            in1=Sb[:, half * 1024 + sub * 512:
                                   half * 1024 + (sub + 1) * 512],
                            op=ALU.mult)
                        v.drain()
                        v.tensor_tensor(
                            out=xout[dc % 2][:, sub * 512:(sub + 1) * 512],
                            in0=tmp[i % 2][:],
                            in1=xch[dc % 2][:, sub * 512:(sub + 1) * 512],
                            op=ALU.add)
                        v.drain()
                        E.inc("vE", v.engine_nop(), 1)

        def f_store(s):
            for half in range(2):
                for dc in range(16):
                    k = half * 16 + dc
                    E.w(s, "vE", b_vE + 2 * (k + 1))
                    E.inc("dC", s.dma_start(
                        out=x_dst[dc * 128:(dc + 1) * 128,
                                  half * 1024:(half + 1) * 1024],
                        in_=xout[dc % 2][:]))

        E.run_phase({"sync": f_sync, "gpsimd": f_gpsimd, "tensor": f_tensor,
                     "vector": f_vector, "scalar": f_store})


def emit_qkv(E, C, xn, wstr, kind, dst_io):
    nc = E.nc
    with ExitStack() as ph:
        row = [ph.enter_context(nc.sbuf_tensor(f"qk{E.phase}_{i}", [128, 2048], BF16))
               for i in range(2)]
        ps = [ph.enter_context(nc.psum_tensor(f"pqk{E.phase}_{i}", [128, 512], F32))
              for i in range(2)]
        b_tE, b_sE, b_dC = E.n["tE"], E.n["sE"], E.n["dC"]
        scale = 0.125 if kind == "q" else 1.0

        def f_tensor(t):
            for oc in range(16):
                for nt in range(4):
                    i = 4 * oc + nt
                    if i >= 2:
                        E.w(t, "sE", b_sE + i - 1)
                    for dc in range(NDC):
                        if kind == "v":
                            mm = t.matmul(
                                out=ps[i % 2][:],
                                lhsT=xn[:, dc * T + oc * 128:dc * T + (oc + 1) * 128],
                                rhs=wstr[:, dc * 2048 + nt * 512:
                                         dc * 2048 + (nt + 1) * 512],
                                start=(dc == 0), stop=(dc == NDC - 1))
                        else:
                            mm = t.matmul(
                                out=ps[i % 2][:],
                                lhsT=wstr[:, dc * 2048 + oc * 128:
                                          dc * 2048 + (oc + 1) * 128],
                                rhs=xn[:, dc * T + nt * 512:dc * T + (nt + 1) * 512],
                                start=(dc == 0), stop=(dc == NDC - 1))
                    E.inc("tE", mm, 1)

        def f_scalar(s):
            for oc in range(16):
                if oc >= 2:
                    E.w(s, "dC", b_dC + 16 * (oc - 1))
                for nt in range(4):
                    i = 4 * oc + nt
                    E.w(s, "tE", b_tE + i + 1)
                    E.inc("sE", s.activation(
                        out=row[oc % 2][:, nt * 512:(nt + 1) * 512],
                        in_=ps[i % 2][:], func=AF.Copy, scale=scale))

        def f_gpsimd(g):
            for oc in range(16):
                E.w(g, "sE", b_sE + 4 * (oc + 1))
                E.inc("dC", g.dma_start(
                    out=dst_io[oc * 128:(oc + 1) * 128, :], in_=row[oc % 2][:]))

        E.run_phase({"tensor": f_tensor, "scalar": f_scalar, "gpsimd": f_gpsimd})


SDPA_LVL = 99


class Stream:

    def __init__(self):
        self.tab = {}
        self.emit = False
        self.cnt = 0
        self.eng = None
        self.E = None
        self.sem = None

    def start(self, E, base):
        self.cnt = base

    def inc(self, inst_fn, key):
        self.cnt += 1
        if self.emit:
            inst = inst_fn()
            self.E.inc(self.sem, inst, 1)
        else:
            self.tab[key] = self.cnt
        return self.cnt


def emit_sdpa(E, C, qio, kio, vio, oio):
    nc = E.nc
    NPAIR = NH // 2
    NGK = NB * NPAIR
    with ExitStack() as ph:
        qsb = [ph.enter_context(nc.sbuf_tensor(f"qsb{E.phase}_{i}", [128, 2048], BF16))
               for i in range(2)]
        ksb = [ph.enter_context(nc.sbuf_tensor(f"ksb{E.phase}_{i}", [128, 4096], BF16))
               for i in range(2)]
        vsb = [ph.enter_context(nc.sbuf_tensor(f"vsb{E.phase}_{i}", [128, 4096], BF16))
               for i in range(2)]
        osb = [ph.enter_context(nc.sbuf_tensor(f"osb{E.phase}_{i}", [128, 2048], BF16))
               for i in range(2)]
        sm = [ph.enter_context(nc.sbuf_tensor(f"sm{E.phase}_{i}", [128, 256], F32))
              for i in range(4)]
        ee = [ph.enter_context(nc.sbuf_tensor(f"ee{E.phase}_{i}", [128, 256], F32))
              for i in range(4)]
        pb = [ph.enter_context(nc.sbuf_tensor(f"pb{E.phase}_{i}", [128, 256], BF16))
              for i in range(4)]
        pT = [ph.enter_context(nc.sbuf_tensor(f"pT{E.phase}_{i}", [128, 512], BF16))
              for i in range(2)]
        nmx = [ph.enter_context(nc.sbuf_tensor(f"nmx{E.phase}_{i}", [128, 1], F32))
               for i in range(4)]
        den = [ph.enter_context(nc.sbuf_tensor(f"den{E.phase}_{i}", [128, 1], F32))
               for i in range(4)]
        rde = [ph.enter_context(nc.sbuf_tensor(f"rde{E.phase}_{i}", [128, 1], F32))
               for i in range(4)]
        psc = [[ph.enter_context(nc.psum_tensor(f"psc{E.phase}_{h}_{i}",
                                                [128, 256], F32))
                for i in range(2)] for h in range(2)]
        ptr = [ph.enter_context(nc.psum_tensor(f"ptr{E.phase}_{i}", [128, 512], BF16))
               for i in range(2)]
        po = [ph.enter_context(nc.psum_tensor(f"po{E.phase}_{i}", [128, 128], F32))
              for i in range(2)]
        b = {k: E.n[k] for k in ("dS", "dC", "tE", "vE", "sE")}
        kview = kio.rearrange("(dqc p) t -> p dqc t", p=128)
        qview = qio.rearrange("(dqc p) t -> p dqc t", p=128)
        oview = oio.rearrange("(dqc p) t -> p dqc t", p=128)

        TS, VS, SS = Stream(), Stream(), Stream()

        def t_stream(t):
            TS.eng = t

            def sc_pair(gk):
                i, k = divmod(gk, NPAIR)

                def go():
                    if TS.emit:
                        E.w(t, "dS", b["dS"] + 16 * 3 * (i + 1))
                        if gk >= 2:
                            E.w(t, "vE", VS.tab[("mx", 2 * (gk - 2) + 1)])
                        if SDPA_LVL < 0:
                            return t.nop()
                        mm = None
                        for h01 in range(2):
                            hh = 2 * gk + h01
                            p0 = 64 * h01
                            dqc = k
                            mm = t.matmul(
                                out=psc[h01][gk % 2][:],
                                lhsT=qsb[i % 2][p0:p0 + 64,
                                                dqc * 128:(dqc + 1) * 128],
                                rhs=ksb[i % 2][p0:p0 + 64,
                                               dqc * 256:(dqc + 1) * 256],
                                start=True, stop=True)
                        return mm
                TS.inc(go, ("sc", gk))

            def tr_pair(gk):
                def go():
                    E.w(t, "sE", SS.tab[("pb", 2 * gk + 1)])
                    if gk >= 2:
                        E.w(t, "vE", VS.tab[("pT", gk - 2)])
                    if SDPA_LVL < 4:
                        return t.nop()
                    mm = None
                    for h01 in range(2):
                        for c in range(2):
                            mm = t.transpose(
                                out=ptr[gk % 2][:, (2 * h01 + c) * 128:
                                                (2 * h01 + c + 1) * 128],
                                in_=pb[(2 * gk + h01) % 4][:, c * 128:(c + 1) * 128],
                                identity=C["eye"][:])
                    return mm
                TS.inc(go, ("tr", gk))

            def pv_pair(gk):
                i, k = divmod(gk, NPAIR)

                def go():
                    E.w(t, "vE", VS.tab[("pT", gk)])
                    if gk >= 2:
                        E.w(t, "vE", VS.tab[("oev", gk - 2)])
                    if SDPA_LVL < 6:
                        return t.nop()
                    mm = None
                    for h01 in range(2):
                        hh = 2 * gk + h01
                        hd = k * 2 + h01
                        for c in range(2):
                            mm = t.matmul(
                                out=po[gk % 2][64 * h01:64 * h01 + 64, :],
                                lhsT=vsb[i % 2][:, c * 2048 + hd * 64:
                                                c * 2048 + hd * 64 + 64],
                                rhs=pT[gk % 2][:, (2 * h01 + c) * 128:
                                               (2 * h01 + c + 1) * 128],
                                start=(c == 0), stop=(c == 1),
                                tile_position=(0, 64 * h01))
                    return mm
                TS.inc(go, ("pv", gk))

            for gk in range(NGK):
                sc_pair(gk)
                if gk >= 1:
                    tr_pair(gk - 1)
                if gk >= 2:
                    pv_pair(gk - 2)
            tr_pair(NGK - 1)
            pv_pair(NGK - 2)
            pv_pair(NGK - 1)

        def v_stream(v):
            VS.eng = v

            def smmx(hh):
                gk = hh // 2
                i = gk // NPAIR

                def go():
                    E.w(v, "tE", TS.tab[("sc", gk)])
                    if hh >= 4:
                        E.w(v, "sE", SS.tab[("exp", hh - 4)])
                    if SDPA_LVL < 1:
                        return v.engine_nop()
                    mask = C["maskB"] if i == 0 else C["maskW"]
                    v.tensor_tensor(out=sm[hh % 4][:],
                                    in0=psc[hh % 2][gk % 2][:],
                                    in1=mask[:], op=ALU.add)
                    v.drain()
                    inst = v.reduce_max(out=nmx[hh % 4][:], in_=sm[hh % 4][:],
                                        axis=AX.X, negate=True)
                    v.drain()
                    return v.engine_nop()
                VS.inc(go, ("mx", hh))

            def rdeh(hh):
                def go():
                    E.w(v, "sE", SS.tab[("exp", hh)])
                    if SDPA_LVL < 2:
                        return v.engine_nop()
                    v.reciprocal(out=rde[hh % 4][:], in_=den[hh % 4][:])
                    v.drain()
                    return v.engine_nop()
                VS.inc(go, ("rde", hh))

            def pTc(gk):
                def go():
                    E.w(v, "tE", TS.tab[("tr", gk)])
                    if SDPA_LVL < 5:
                        return v.engine_nop()
                    inst = v.tensor_copy(out=pT[gk % 2][:], in_=ptr[gk % 2][:])
                    v.drain()
                    return v.engine_nop()
                VS.inc(go, ("pT", gk))

            def oev(gk):
                i, k = divmod(gk, NPAIR)

                def go():
                    E.w(v, "tE", TS.tab[("pv", gk)])
                    if k == 0 and i >= 2:
                        E.w(v, "dC", b["dC"] + 16 * (i - 1))
                    if SDPA_LVL < 7:
                        return v.engine_nop()
                    inst = v.tensor_copy(out=osb[i % 2][:, k * 128:(k + 1) * 128],
                                         in_=po[gk % 2][:])
                    v.drain()
                    return v.engine_nop()
                VS.inc(go, ("oev", gk))

            def z():
                v.memset(ksb[0][:], 0.0)
                v.memset(vsb[0][:], 0.0)
                v.drain()
                return v.engine_nop()
            VS.inc(z, ("zero",))

            for gk in range(NGK):
                smmx(2 * gk)
                smmx(2 * gk + 1)
                if gk >= 1:
                    rdeh(2 * (gk - 1))
                    rdeh(2 * (gk - 1) + 1)
                    pTc(gk - 1)
                if gk >= 2:
                    oev(gk - 2)
            rdeh(2 * (NGK - 1))
            rdeh(2 * (NGK - 1) + 1)
            pTc(NGK - 1)
            oev(NGK - 2)
            oev(NGK - 1)

        def s_stream(s):
            SS.eng = s

            def exph(hh):
                def go():
                    E.w(s, "vE", VS.tab[("mx", hh)])
                    if hh >= 4:
                        E.w(s, "vE", VS.tab[("rde", hh - 4)])
                    if SDPA_LVL < 2:
                        return s.nop()
                    inst = s.activation(out=ee[hh % 4][:], in_=sm[hh % 4][:],
                                        func=AF.Exp, bias=nmx[hh % 4].ap()[:, 0:1],
                                        accum_out=den[hh % 4][:])
                    s.drain()
                    return s.nop()
                SS.inc(go, ("exp", hh))

            def pbh(hh):
                gk = hh // 2

                def go():
                    E.w(s, "vE", VS.tab[("rde", hh)])
                    if gk >= 2:
                        E.w(s, "tE", TS.tab[("tr", gk - 2)])
                    if SDPA_LVL < 3:
                        return s.nop()
                    inst = s.activation(out=pb[hh % 4][:], in_=ee[hh % 4][:],
                                        func=AF.Copy, scale=rde[hh % 4].ap()[:, 0:1])
                    s.drain()
                    return s.nop()
                SS.inc(go, ("pb", hh))

            for gk in range(NGK):
                if gk >= 1:
                    exph(2 * (gk - 1))
                    pbh(2 * (gk - 1))
                    exph(2 * (gk - 1) + 1)
                    pbh(2 * (gk - 1) + 1)
            exph(2 * (NGK - 1))
            pbh(2 * (NGK - 1))
            exph(2 * (NGK - 1) + 1)
            pbh(2 * (NGK - 1) + 1)

        def f_sync(y):
            for i in range(NB):
                if i >= 2:
                    E.w(y, "tE", TS.tab[("pv", NPAIR * (i - 1) - 1)])
                if i == 0:
                    E.w(y, "vE", VS.tab[("zero",)])
                    E.inc("dS", y.dma_start(
                        out=qsb[0].ap().rearrange("p (dqc c) -> p dqc c", c=128),
                        in_=qview[:, :, 0:128]))
                    E.inc("dS", y.dma_start(
                        out=ksb[0].ap().rearrange("p (dqc c) -> p dqc c", c=256)[
                            :, :, 128:256],
                        in_=kview[:, :, 0:128]))
                    E.inc("dS", y.dma_start(
                        out=vsb[0][:, 2048:4096], in_=vio[0:128, :]))
                else:
                    E.inc("dS", y.dma_start(
                        out=qsb[i % 2].ap().rearrange("p (dqc c) -> p dqc c", c=128),
                        in_=qview[:, :, i * 128:(i + 1) * 128]))
                    E.inc("dS", y.dma_start(
                        out=ksb[i % 2].ap().rearrange("p (dqc c) -> p dqc c", c=256),
                        in_=kview[:, :, (i - 1) * 128:(i + 1) * 128]))
                    E.inc("dS", y.dma_start(
                        out=vsb[i % 2].ap().rearrange("p (c d) -> p c d", c=2),
                        in_=vio[(i - 1) * 128:(i + 1) * 128, :].rearrange(
                            "(c p) d -> p c d", p=128)))

        def f_store(g):
            for i in range(NB):
                E.w(g, "vE", VS.tab[("oev", NPAIR * (i + 1) - 1)])
                E.inc("dC", g.dma_start(
                    out=oview[:, :, i * 128:(i + 1) * 128],
                    in_=osb[i % 2].ap().rearrange("p (dqc c) -> p dqc c", c=128)))

        for st, base in ((TS, E.n["tE"]), (VS, E.n["vE"]), (SS, E.n["sE"])):
            st.E = E
            st.emit = False
        TS.sem, VS.sem, SS.sem = "tE", "vE", "sE"
        TS.start(E, E.n["tE"])
        t_stream(None)
        VS.start(E, E.n["vE"])
        v_stream(None)
        SS.start(E, E.n["sE"])
        s_stream(None)

        def f_tensor(t):
            TS.emit = True
            TS.cnt = b["tE"]
            t_stream(t)

        def f_vector(v):
            VS.emit = True
            VS.cnt = b["vE"]
            v_stream(v)

        def f_scalar(s):
            SS.emit = True
            SS.cnt = b["sE"]
            s_stream(s)

        E.run_phase({"tensor": f_tensor, "vector": f_vector, "scalar": f_scalar,
                     "sync": f_sync, "gpsimd": f_store})


def emit_outproj(E, C, oio, wostr, x_src, x_dst):
    nc = E.nc
    with ExitStack() as ph:
        opan = ph.enter_context(nc.sbuf_tensor(f"opan{E.phase}", [128, 16384], BF16))
        xch = [ph.enter_context(nc.sbuf_tensor(f"oxch{E.phase}_{i}", [128, 1024], F32))
               for i in range(2)]
        xout = [ph.enter_context(nc.sbuf_tensor(f"oxo{E.phase}_{i}", [128, 1024], F32))
                for i in range(2)]
        ps_y = [ph.enter_context(nc.psum_tensor(f"poy{E.phase}_{i}", [128, 512], F32))
                for i in range(2)]
        b_dS, b_dC, b_tE, b_vE = E.n["dS"], E.n["dC"], E.n["tE"], E.n["vE"]
        oview = oio.rearrange("(dvc p) t -> p dvc t", p=128)

        def f_sync(y):
            ndma = 0
            for half in range(2):
                if half == 1:
                    E.w(y, "tE", b_tE + 32)
                for q in range(4):
                    E.inc("dS", y.dma_start(
                        out=opan.ap().rearrange("p (dvc c) -> p dvc c", c=1024)[
                            :, q * 4:(q + 1) * 4],
                        in_=oview[:, q * 4:(q + 1) * 4,
                                  half * 1024:(half + 1) * 1024]))
                for dc in range(16):
                    k = half * 16 + dc
                    if k >= 2:
                        E.w(y, "vE", b_vE + 2 * (k - 1))
                    E.inc("dS", y.dma_start(
                        out=xch[dc % 2][:],
                        in_=x_src[dc * 128:(dc + 1) * 128,
                                  half * 1024:(half + 1) * 1024]))

        def f_tensor(t):
            for half in range(2):
                E.w(t, "dS", b_dS + 16 * (4 if half == 0 else 24))
                for dc in range(16):
                    k = half * 16 + dc
                    for sub in range(2):
                        i = 2 * k + sub
                        if i >= 2:
                            E.w(t, "vE", b_vE + i - 1)
                        for dvc in range(16):
                            mm = t.matmul(
                                out=ps_y[i % 2][:],
                                lhsT=wostr[:, dvc * 2048 + dc * 128:
                                           dvc * 2048 + (dc + 1) * 128],
                                rhs=opan[:, dvc * 1024 + sub * 512:
                                         dvc * 1024 + (sub + 1) * 512],
                                start=(dvc == 0), stop=(dvc == 15))
                        E.inc("tE", mm, 1)

        def f_vector(v):
            for half in range(2):
                for dc in range(16):
                    k = half * 16 + dc
                    E.w(v, "dS", b_dS + 16 * (4 + 4 * half + k + 1))
                    if k >= 2:
                        E.w(v, "dC", b_dC + 16 * (k - 1))
                    for sub in range(2):
                        i = 2 * k + sub
                        E.w(v, "tE", b_tE + i + 1)
                        v.tensor_tensor(
                            out=xout[dc % 2][:, sub * 512:(sub + 1) * 512],
                            in0=ps_y[i % 2][:],
                            in1=xch[dc % 2][:, sub * 512:(sub + 1) * 512],
                            op=ALU.add)
                        v.drain()
                        E.inc("vE", v.engine_nop(), 1)

        def f_store(s):
            for half in range(2):
                for dc in range(16):
                    k = half * 16 + dc
                    E.w(s, "vE", b_vE + 2 * (k + 1))
                    E.inc("dC", s.dma_start(
                        out=x_dst[dc * 128:(dc + 1) * 128,
                                  half * 1024:(half + 1) * 1024],
                        in_=xout[dc % 2][:]))

        E.run_phase({"sync": f_sync, "tensor": f_tensor, "vector": f_vector,
                     "scalar": f_store})


def emit_debug_store(E, name, sbuf_ap, shape, dtype):
    d = E.nc.dram_tensor("dbg_" + name, shape, dtype, kind="ExternalOutput").ap()

    def f_sync(y):
        n = E.inc("dS", y.dma_start(out=d[:], in_=sbuf_ap))
        E.w(y, "dS", n)

    E.run_phase({"sync": f_sync})


def build_nc(stages, dump=()):
    nc = bass.Bass("TRN2", target_bir_lowering=False, debug=False)
    io = {}

    def dram(name, shape, dtype, kind):
        h = nc.dram_tensor(name, shape, dtype, kind=kind)
        io[name] = h.ap()
        return io[name]

    dram("xT", [D, T], F32, "ExternalInput")
    dram("Wq", [D, D], F32, "ExternalInput")
    dram("Wk", [D, D], F32, "ExternalInput")
    dram("Wv", [D, D], F32, "ExternalInput")
    dram("Wo", [D, D], F32, "ExternalInput")
    dram("Wgu", [H2, D], F32, "ExternalInput")
    dram("Wp", [D, HF], F32, "ExternalInput")
    dram("eye", [128, 128], BF16, "ExternalInput")
    dram("ones_col", [128, 1], BF16, "ExternalInput")
    dram("ones_row", [1, 128], F32, "ExternalInput")
    dram("maskW", [128, 256], F32, "ExternalInput")
    dram("maskB", [128, 256], F32, "ExternalInput")
    dram("outT", [D, T], F32, "ExternalOutput")

    def scr(name, shape, dtype):
        kind = "ExternalOutput" if name in dump else "Internal"
        return dram(name, shape, dtype, kind)

    scr("wgu_nat", [H2, D], BF16)
    scr("watt_nat", [D, D], BF16)
    scr("wp_nat", [D, HF], BF16)
    scr("wguT", [NDC, 128, H2], BF16)
    scr("wpT", [NHC, 128, D], BF16)
    scr("h_scr", [HF, T], BF16)
    scr("qT_scr", [D, T], BF16)
    scr("kT_scr", [D, T], BF16)
    scr("v_scr", [T, D], BF16)
    scr("oT_scr", [D, T], BF16)
    scr("xA", [D, T], F32)
    scr("xB", [D, T], F32)

    with ExitStack() as ctx:
        with nc.Block() as block:
            E = Env(nc, block, ctx)
            C = {}
            C["ones_col_bf"] = ctx.enter_context(
                nc.sbuf_tensor("c_ones_col", [128, 1], BF16))
            C["ones_row_f32"] = ctx.enter_context(
                nc.sbuf_tensor("c_ones_row", [1, 128], F32))
            C["eye"] = ctx.enter_context(nc.sbuf_tensor("c_eye", [128, 128], BF16))
            C["maskW"] = ctx.enter_context(
                nc.sbuf_tensor("c_maskW", [128, 256], F32))
            C["maskB"] = ctx.enter_context(
                nc.sbuf_tensor("c_maskB", [128, 256], F32))

            def f_consts(y):
                for nm, key in [("eye", "eye"), ("ones_col", "ones_col_bf"),
                                ("ones_row", "ones_row_f32"),
                                ("maskW", "maskW"), ("maskB", "maskB")]:
                    E.inc("dS", y.dma_start(out=C[key][:], in_=io[nm][:]))
                E.w(y, "dS", E.n["dS"])

            E.run_phase({"sync": f_consts})

            run_stages(E, C, io, stages, dump)

            def f_fin(y):
                E.w(y, "dS", E.n["dS"])
                E.w(y, "dG", E.n["dG"])
                E.w(y, "dC", E.n["dC"])

            E.run_phase({"sync": f_fin})

    return nc, io


def run_stages(E, C, io, stages, dump):
    nc = E.nc

    if "ln1" in stages:
        with ExitStack() as sl:
            xn = sl.enter_context(
                nc.sbuf_tensor("xn1", [128, NDC * T], BF16))
            emit_ln(E, C, io["xT"], xn.ap())
            if "xn1" in dump:
                emit_debug_store(E, "xn1", xn.ap()[:],
                                 [128, NDC * T], BF16)

    if "quant" in stages:
        blocks = []
        for r in range(H2 // 128):
            blocks.append((io["Wgu"][r * 128:(r + 1) * 128, :],
                           io["wgu_nat"][r * 128:(r + 1) * 128, :]))
        for r in range(D // 128):
            for c in range(HF // 2048):
                blocks.append(
                    (io["Wp"][r * 128:(r + 1) * 128, c * 2048:(c + 1) * 2048],
                     io["wp_nat"][r * 128:(r + 1) * 128, c * 2048:(c + 1) * 2048]))
        emit_quant(E, C, blocks, None)

    if "xbar_ffn" in stages:
        jobs = []
        for dc in range(NDC):
            jobs.append((io["wgu_nat"][:, dc * 128:(dc + 1) * 128],
                         io["wguT"][dc], H2))
        emit_xbar(E, C, jobs)
        jobs = []
        for hc in range(NHC):
            jobs.append((io["wp_nat"][:, hc * 128:(hc + 1) * 128],
                         io["wpT"][hc], D))
        emit_xbar(E, C, jobs)

    def ffn_sublayer(tag, x_src, x_dst):
        with ExitStack() as s0:
            Sb = s0.enter_context(nc.sbuf_tensor(f"Sb_{tag}", [128, T], F32))
            ps_ssq_scope = ExitStack()
            with ExitStack() as s1:
                xn = s1.enter_context(
                    nc.sbuf_tensor(f"xn_{tag}", [128, NDC * T], BF16))
                emit_ln(E, C, x_src, xn.ap())
                ps_ssq = [ps_ssq_scope.enter_context(
                    nc.psum_tensor(f"pssq_{tag}_{i}", [1, 512], F32))
                    for i in range(4)]
                emit_gu(E, C, xn.ap(), io["wguT"], io["h_scr"], ps_ssq)
            emit_rms(E, C, ps_ssq, Sb.ap(), 0.5)
            ps_ssq_scope.close()
            emit_proj(E, C, io["h_scr"], io["wpT"], Sb.ap(), x_src, x_dst)

    if "ffn1" in stages:
        ffn_sublayer("f1", io["xT"], io["xA"])

    def wprep(w_src, strips_ap_list):
        def f_cast(g):
            n = E.inc("dG", g.dma_start(out=io["watt_nat"][:], in_=w_src[:]))
        E.run_phase({"gpsimd": f_cast})
        emit_xbar(E, C, [(io["watt_nat"][:, dc * 128:(dc + 1) * 128], None, D)
                         for dc in range(NDC)], sb_strips=strips_ap_list)

    if "attn" in stages or "attn_qkv" in stages or "attn_sdpa" in stages:
        with ExitStack() as sa:
            with ExitStack() as sb1:
                xn = sb1.enter_context(
                    nc.sbuf_tensor("xn_at", [128, NDC * T], BF16))
                wstr = sb1.enter_context(
                    nc.sbuf_tensor("wstr", [128, 32768], BF16,
                                   is_transpose_dest=True))
                emit_ln(E, C, io["xA"], xn.ap())
                wslices = [wstr.ap()[:, dc * 2048:(dc + 1) * 2048]
                           for dc in range(NDC)]
                for wname, kind, dst in (("Wq", "q", "qT_scr"),
                                         ("Wk", "k", "kT_scr"),
                                         ("Wv", "v", "v_scr")):
                    wprep(io[wname], wslices)
                    emit_qkv(E, C, xn.ap(), wstr.ap(), kind, io[dst])
            if "attn" in stages or "attn_sdpa" in stages:
                emit_sdpa(E, C, io["qT_scr"], io["kT_scr"], io["v_scr"],
                          io["oT_scr"])
            if "attn" in stages:
                wostr = sa.enter_context(
                    nc.sbuf_tensor("wostr", [128, 32768], BF16,
                                   is_transpose_dest=True))
                wprep(io["Wo"], [wostr.ap()[:, dc * 2048:(dc + 1) * 2048]
                                 for dc in range(NDC)])
                emit_outproj(E, C, io["oT_scr"], wostr.ap(), io["xA"],
                             io["xB"])

    if "ffn2" in stages:
        ffn_sublayer("f2", io["xB"], io["outT"])


def _consts_host():
    r = np.arange(BS)[:, None]
    c2 = np.arange(2 * BS)[None, :]
    maskW = np.where((c2 >= r) & (c2 <= r + BS), np.float32(0), np.float32(-1e38))
    maskB = np.full((BS, 2 * BS), np.float32(-1e38))
    cr = np.arange(BS)[None, :]
    maskB[:, BS:] = np.where(cr <= r, np.float32(0), np.float32(-1e38))
    return {
        "eye": np.eye(128, dtype=bfnp),
        "ones_col": np.ones((128, 1), dtype=bfnp),
        "ones_row": np.ones((1, 128), np.float32),
        "maskW": maskW.astype(np.float32),
        "maskB": maskB.astype(np.float32),
    }


def kernel(L0_attn_x, L0_attn_Wq, L0_attn_Wk, L0_attn_Wv, L0_attn_Wo,
           L0_ffn_w_gate_up, L0_ffn_w_proj):
    stages = ["ln1", "quant", "xbar_ffn", "ffn1", "attn", "ffn2"]
    nc, io = build_nc(stages)
    consts = _consts_host()
    shared = {
        "Wq": np.ascontiguousarray(L0_attn_Wq, np.float32),
        "Wk": np.ascontiguousarray(L0_attn_Wk, np.float32),
        "Wv": np.ascontiguousarray(L0_attn_Wv, np.float32),
        "Wo": np.ascontiguousarray(L0_attn_Wo, np.float32),
        "Wgu": np.ascontiguousarray(L0_ffn_w_gate_up, np.float32),
        "Wp": np.ascontiguousarray(L0_ffn_w_proj, np.float32),
        **consts,
    }
    in_maps = []
    for b in range(8):
        m = dict(shared)
        m["xT"] = np.ascontiguousarray(np.asarray(L0_attn_x[b], np.float32).T)
        in_maps.append(m)
    res = run_bass_kernel_spmd(nc, in_maps, core_ids=list(range(8)))
    out = np.stack([np.asarray(r["outT"]).T for r in res.results])
    return np.ascontiguousarray(out, dtype=np.float32)
